# version 01
# baseline (speedup 1.0000x reference)
import numpy as np
import ml_dtypes
from contextlib import ExitStack

import concourse.bass as bass
import concourse.mybir as mybir
from concourse.bass_utils import run_bass_kernel_spmd

F32 = mybir.dt.float32
BF16 = mybir.dt.bfloat16
AF = mybir.ActivationFunctionType
ALU = mybir.AluOpType
AXX = mybir.AxisListType.X
BFDT = ml_dtypes.bfloat16

B, S, D = 2, 4096, 4096
HPC = 8
NCH = 8
DCN = 32
CSHIFT = 16.0
SCALE = 1.0 / float(np.sqrt(128.0))
NEG = -1.0e9


def _slopes():
    m = np.arange(1, 33, dtype=np.float64) * (8.0 / 32.0)
    return 2.0 ** (-m)


def _atab_offsets():
    off = {}
    cur = 0
    for hp in range(HPC):
        for c in range(NCH):
            for sub in range(4):
                off[(hp, c, sub)] = cur
                cur += 4 * c + 4
    return off, cur


_ATAB_OFF, _ATAB_N = _atab_offsets()


def _host_prep_shared(inputs):
    d = {}
    x = np.asarray(inputs["L0_attn_x"])
    for g in range(B):
        xT = np.ascontiguousarray(x[g].T)
        d[f"x_t{g}"] = np.ascontiguousarray(
            xT.reshape(DCN, 128, 8, 512).transpose(2, 1, 0, 3))
    wqkv = np.asarray(inputs["L0_attn_w_qkv"])
    bqkv = np.asarray(inputs["L0_attn_b_qkv"])
    for j in range(4):
        for nm, base in (("wq", 0), ("wk", D), ("wv", 2 * D)):
            sl = wqkv[base + 1024 * j: base + 1024 * j + 1024]
            if nm == "wv":
                t = sl.reshape(1024, DCN, 128).transpose(1, 2, 0)
            else:
                t = sl.reshape(8, 128, DCN, 128).transpose(0, 3, 2, 1)
            d[f"{nm}_t{j}"] = np.ascontiguousarray(t)
        bq = bqkv[1024 * j: 1024 * j + 1024].reshape(8, 128).T
        bk = bqkv[D + 1024 * j: D + 1024 * j + 1024].reshape(8, 128).T
        d[f"bqk{j}"] = np.ascontiguousarray(np.concatenate([bq, bk], axis=1))
        bv = bqkv[2 * D + 1024 * j: 2 * D + 1024 * j + 1024]
        d[f"bv{j}"] = np.ascontiguousarray(
            np.broadcast_to(bv[None, :], (128, 1024)).astype(np.float32))
        sl_h = _slopes()[8 * j: 8 * j + 8]
        at = np.zeros((128, _ATAB_N), dtype=np.float64)
        p = np.arange(128, dtype=np.float64)
        for hp in range(HPC):
            slope = sl_h[hp]
            for c in range(NCH):
                for sub in range(4):
                    bse = _ATAB_OFF[(hp, c, sub)]
                    qmid = 512 * c + 128 * sub + 64
                    for t_ in range(4 * c + 4):
                        if t_ > 4 * c + sub:
                            at[:, bse + t_] = NEG
                        else:
                            at[:, bse + t_] = slope * (128 * t_ + p - qmid) - CSHIFT
        d[f"atab{j}"] = np.ascontiguousarray(at.astype(np.float32))
    tri = (np.arange(128)[None, :] >= np.arange(128)[:, None])
    d["trimask"] = np.ascontiguousarray(tri.astype(BFDT))
    wout = np.asarray(inputs["L0_attn_w_out"])
    d["wo_t"] = np.ascontiguousarray(
        wout.reshape(32, 128, DCN, 128).transpose(0, 3, 2, 1))
    wffn = np.asarray(inputs["L0_ffn_W"])
    d["wf_t"] = np.ascontiguousarray(
        wffn.reshape(32, 128, DCN, 128).transpose(0, 3, 2, 1))
    d["bo_c"] = np.ascontiguousarray(np.asarray(inputs["L0_attn_b_out"]).reshape(32, 128).T)
    d["bf_c"] = np.ascontiguousarray(np.asarray(inputs["L0_ffn_b"]).reshape(32, 128).T)
    A = np.asarray(inputs["L0_ffn_A"])
    at_p = np.zeros((DCN, 128, 128), dtype=np.float32)
    at_p[:, :, :64] = A.T.reshape(DCN, 128, 64)
    d["at_t"] = np.ascontiguousarray(at_p)
    Bm = np.asarray(inputs["L0_ffn_B"])
    bt_p = np.zeros((32, 128, 128), dtype=np.float32)
    bt_p[:, :64, :] = Bm.reshape(32, 128, 64).transpose(0, 2, 1)
    d["bt_t"] = np.ascontiguousarray(bt_p)
    d["gw_t"] = np.ascontiguousarray(
        np.asarray(inputs["L0_ffn_gate_w"]).T.reshape(DCN, 128, 4))
    d["gb_row"] = np.ascontiguousarray(np.asarray(inputs["L0_ffn_gate_b"]).reshape(1, 4))
    d["iota"] = np.ascontiguousarray(np.broadcast_to(np.arange(128, dtype=np.float32)[:, None], (128, 512)).copy())
    d["ident"] = np.ascontiguousarray(np.eye(128, dtype=np.float32))
    for g in range(B):
        xT = np.ascontiguousarray(x[g].T)
        for j in range(4):
            sl = xT[:, 1024 * j: 1024 * j + 1024]
            d[f"xq{g}_{j}"] = np.ascontiguousarray(sl.reshape(DCN, 128, 1024))
    return d


class Em:
    def __init__(self, nc, ctx):
        self.nc = nc
        self.sem = {}
        self.cnt = {}
        for name in ("gp", "sy", "mm", "act", "dve"):
            self.sem[name] = ctx.enter_context(nc.semaphore("s_" + name))
            self.cnt[name] = 0

    def inc(self, inst, name, n=1):
        inst.then_inc(self.sem[name], n)
        self.cnt[name] += n
        return self.cnt[name]

    def w(self, eng, name, val=None):
        v = self.cnt[name] if val is None else val
        if v > 0:
            eng.wait_ge(self.sem[name], v)


def build_l1():
    nc = bass.Bass("TRN2", target_bir_lowering=False)
    x_t = nc.dram_tensor("x_t", [8, 128, DCN, 512], F32, kind="ExternalInput")
    wq_t = nc.dram_tensor("wq_t", [8, 128, DCN, 128], F32, kind="ExternalInput")
    wk_t = nc.dram_tensor("wk_t", [8, 128, DCN, 128], F32, kind="ExternalInput")
    wv_t = nc.dram_tensor("wv_t", [DCN, 128, 1024], F32, kind="ExternalInput")
    bqk = nc.dram_tensor("bqk", [128, 16], F32, kind="ExternalInput")
    bv = nc.dram_tensor("bv", [128, 1024], F32, kind="ExternalInput")
    atab = nc.dram_tensor("atab", [128, _ATAB_N], F32, kind="ExternalInput")
    trimask = nc.dram_tensor("trimask", [128, 128], BF16, kind="ExternalInput")
    contrib = nc.dram_tensor("contrib", [1024, S], BF16, kind="ExternalOutput")
    QT = nc.dram_tensor("QT", [HPC, 128, S], BF16, kind="Internal")
    KT = nc.dram_tensor("KT", [HPC, 128, S], BF16, kind="Internal")
    VD = nc.dram_tensor("VD", [HPC, 128, DCN, 128], BF16, kind="Internal")

    with ExitStack() as ctx:
        E = ctx.enter_context
        big = E(nc.sbuf_tensor("big", [128, 49152], BF16))
        qh = E(nc.sbuf_tensor("qh", [128, S], BF16))
        kh = E(nc.sbuf_tensor("kh", [128, S], BF16))
        vh = E(nc.sbuf_tensor("vh", [128, DCN * 128], BF16))
        atab_sb = E(nc.sbuf_tensor("atab_sb", [128, _ATAB_N], F32))
        tri_sb = E(nc.sbuf_tensor("tri_sb", [128, 128], BF16))
        bqk_sb = E(nc.sbuf_tensor("bqk_sb", [128, 16], F32))
        bv_sb = E(nc.sbuf_tensor("bv_sb", [128, 1024], F32))
        ev_sb = E(nc.sbuf_tensor("ev_sb", [128, 512 * 4], BF16))
        exp_sb = E(nc.sbuf_tensor("exp_sb", [128, 512 * 2], BF16))
        ones_bf = E(nc.sbuf_tensor("ones_bf", [128, 128], BF16))
        ones1f = E(nc.sbuf_tensor("ones1f", [1, 128], F32))
        recip_sb = E(nc.sbuf_tensor("recip_sb", [1, 512 * 2], F32))
        rep_sb = E(nc.sbuf_tensor("rep_sb", [128, 512 * 2], F32))
        att_sb = E(nc.sbuf_tensor("att_sb", [128, 512 * 2], BF16))
        ps = [E(nc.psum_tensor(f"ps{i}", [128, 512], F32)) for i in range(8)]
        em = Em(nc, ctx)
        t, v, sc, gp, sy = nc.tensor, nc.vector, nc.scalar, nc.gpsimd, nc.sync
        sc.preload_act_table(AF.Exp)
        gp.memset(ones_bf[:, :], 1.0)
        gp.memset(ones1f[:, :], 1.0)
        em.inc(gp.dma_start(out=bqk_sb[:, :], in_=bqk[:, :]), "gp", 16)
        em.inc(gp.dma_start(out=bv_sb[:, :], in_=bv[:, :]), "gp", 16)
        em.inc(gp.dma_start(out=atab_sb[:, :], in_=atab[:, :]), "gp", 16)
        c_tri = em.inc(gp.dma_start(out=tri_sb[:, :], in_=trimask[:, :]), "gp", 16)

        XQ, WQH = 0, 32768
        evp = 0
        ev_guard = [0, 0, 0, 0]
        bank_guard = [0] * 8
        big3 = lambda lo, n: big[:, lo:lo + n]
        for q in range(4):
            em.w(gp, "mm")
            for half in range(2):
                dst = big[:, XQ + half * 16384: XQ + (half + 1) * 16384] \
                    .rearrange("p (a b) -> p a b", a=DCN)
                lx = em.inc(gp.dma_start(out=dst, in_=x_t[2 * q + half]), "gp", 16)
            for ws in range(4):
                wt = wq_t if ws < 2 else wk_t
                oph0 = (ws % 2) * 4
                em.w(gp, "mm")
                wdst = big[:, WQH:WQH + 16384].rearrange("p (a b c) -> p a b c", a=4, b=DCN)
                lw = em.inc(gp.dma_start(out=wdst,
                                         in_=wt[oph0:oph0 + 4].transpose([1, 0, 2, 3])),
                            "gp", 16)
                for op in range(4):
                    opg = oph0 + op
                    for tc in range(2):
                        pb = (op * 2 + tc) % 4
                        em.w(t, "act", bank_guard[pb])
                        em.w(t, "gp", lw)
                        for dc in range(DCN):
                            mm = t.matmul(
                                ps[pb][:, :],
                                big3(WQH + (op * DCN + dc) * 128, 128),
                                big3(XQ + (tc * DCN + dc) * 512, 512),
                                start=(dc == 0), stop=(dc == DCN - 1))
                        mdone = em.inc(mm, "mm")
                        em.w(sc, "mm", mdone)
                        em.w(sc, "sy", ev_guard[evp])
                        bc = (0 if ws < 2 else 8) + opg
                        a = sc.activation(ev_sb[:, evp * 512:(evp + 1) * 512],
                                          ps[pb][:, :], AF.Identity,
                                          bias=bqk_sb[:, bc:bc + 1], scale=1.0)
                        bank_guard[pb] = em.inc(a, "act")
                        em.w(sy, "act", bank_guard[pb])
                        dstp = (QT if ws < 2 else KT)
                        st = sy.dma_start(
                            out=dstp[opg, :, q * 1024 + tc * 512: q * 1024 + tc * 512 + 512],
                            in_=ev_sb[:, evp * 512:(evp + 1) * 512])
                        ev_guard[evp] = em.inc(st, "sy", 16)
                        evp = (evp + 1) % 4

        WV, XC = 0, 32768
        em.w(gp, "mm")
        wvdst = big[:, WV:WV + DCN * 1024].rearrange("p (a b) -> p a b", a=DCN)
        em.inc(gp.dma_start(out=wvdst, in_=wv_t.ap().transpose([1, 0, 2])), "gp", 16)
        vb_guard = [0] * 8
        for ch in range(8):
            em.w(gp, "mm")
            xdst = big[:, XC:XC + 16384].rearrange("p (a b) -> p a b", a=DCN)
            lx = em.inc(gp.dma_start(out=xdst, in_=x_t[ch]), "gp", 16)
            for tt in range(4):
                for oc in range(2):
                    pb = tt * 2 + oc
                    em.w(t, "dve", vb_guard[pb])
                    em.w(t, "gp", lx)
                    for dc in range(DCN):
                        mm = t.matmul(
                            ps[pb][:, :],
                            big3(XC + dc * 512 + tt * 128, 128),
                            big3(WV + dc * 1024 + oc * 512, 512),
                            start=(dc == 0), stop=(dc == DCN - 1))
                    mdone = em.inc(mm, "mm")
                    em.w(v, "mm", mdone)
                    evs = oc * 2
                    em.w(v, "sy", ev_guard[evs])
                    tt_ = v.tensor_tensor(ev_sb[:, evs * 512:(evs + 1) * 512],
                                          ps[pb][:, :],
                                          bv_sb[:, oc * 512:oc * 512 + 512], ALU.add)
                    vb_guard[pb] = em.inc(tt_, "dve")
                    em.w(sy, "dve", vb_guard[pb])
                    kt_idx = ch * 4 + tt
                    src = ev_sb[:, evs * 512:(evs + 1) * 512] \
                        .rearrange("p (h d) -> p h d", h=4)
                    st = sy.dma_start(
                        out=VD[oc * 4:oc * 4 + 4, :, kt_idx, :].transpose([1, 0, 2]),
                        in_=src)
                    ev_guard[evs] = em.inc(st, "sy", 16)

        em.w(gp, "sy")
        em.w(t, "gp", c_tri)
        exp_guard = [0, 0]
        sc_guard = [0, 0]
        av_guard = [0, 0]
        dn_guard = [0, 0]
        rep_guard = 0
        att_guard = [0, 0]
        rec_guard = [0, 0]
        for hp in range(HPC):
            em.w(gp, "mm")
            lq = em.inc(gp.dma_start(out=qh[:, :], in_=QT[hp]), "gp", 16)
            lk = em.inc(gp.dma_start(out=kh[:, :], in_=KT[hp]), "gp", 16)
            lv = em.inc(gp.dma_start(
                out=vh[:, :], in_=VD[hp].rearrange("p a b -> p (a b)")), "gp", 16)
            em.w(t, "gp", lv)
            for c in range(NCH):
                T = 4 * c + 4
                cp = c % 2
                for t_ in range(T):
                    sb = t_ % 2
                    em.w(t, "act", sc_guard[sb])
                    mm = t.matmul(ps[sb][:, :], kh[:, t_ * 128:(t_ + 1) * 128],
                                  qh[:, c * 512:(c + 1) * 512], start=True, stop=True)
                    smdone = em.inc(mm, "mm")
                    em.w(sc, "mm", smdone)
                    em.w(sc, "mm", exp_guard[sb])
                    for sub in range(4):
                        col = _ATAB_OFF[(hp, c, sub)] + t_
                        a = sc.activation(
                            exp_sb[:, sb * 512 + sub * 128: sb * 512 + (sub + 1) * 128],
                            ps[sb][:, sub * 128:(sub + 1) * 128], AF.Exp,
                            bias=atab_sb[:, col:col + 1], scale=SCALE)
                    adone = em.inc(a, "act")
                    sc_guard[sb] = adone
                    dep = ("act", adone)
                    if t_ >= 4 * c:
                        sd = t_ - 4 * c
                        em.w(v, "act", adone)
                        mt = v.tensor_tensor(
                            exp_sb[:, sb * 512 + sd * 128: sb * 512 + (sd + 1) * 128],
                            exp_sb[:, sb * 512 + sd * 128: sb * 512 + (sd + 1) * 128],
                            tri_sb[:, :], ALU.mult)
                        dep = ("dve", em.inc(mt, "dve"))
                    em.w(t, dep[0], dep[1])
                    if t_ == 0:
                        em.w(t, "dve", av_guard[cp])
                        em.w(t, "dve", dn_guard[cp])
                    t.matmul(ps[2 + cp][:, :], vh[:, t_ * 128:(t_ + 1) * 128],
                             exp_sb[:, sb * 512: sb * 512 + 512],
                             start=(t_ == 0), stop=(t_ == T - 1))
                    mm2 = t.matmul(ps[4 + cp][:, :], ones_bf[:, :],
                                   exp_sb[:, sb * 512: sb * 512 + 512],
                                   start=(t_ == 0), stop=(t_ == T - 1))
                    exp_guard[sb] = em.inc(mm2, "mm")
                em.w(v, "mm")
                em.w(v, "mm", rec_guard[cp])
                r = v.reciprocal(recip_sb[0:1, cp * 512: cp * 512 + 512],
                                 ps[4 + cp][0:1, :])
                rdone = em.inc(r, "dve")
                dn_guard[cp] = rdone
                em.w(t, "dve", rdone)
                em.w(t, "dve", rep_guard)
                mmr = t.matmul(ps[6][:, :], ones1f[0:1, :],
                               recip_sb[0:1, cp * 512: cp * 512 + 512],
                               start=True, stop=True)
                rec_guard[cp] = em.inc(mmr, "mm")
                em.w(v, "mm", rec_guard[cp])
                cpy = v.tensor_copy(rep_sb[:, cp * 512: cp * 512 + 512], ps[6][:, :])
                rep_guard = em.inc(cpy, "dve")
                em.w(v, "sy", att_guard[cp])
                ttn = v.tensor_tensor(att_sb[:, cp * 512: cp * 512 + 512],
                                      ps[2 + cp][:, :],
                                      rep_sb[:, cp * 512: cp * 512 + 512], ALU.mult)
                av_guard[cp] = em.inc(ttn, "dve")
                em.w(sy, "dve", av_guard[cp])
                st = sy.dma_start(
                    out=contrib[hp * 128:(hp + 1) * 128, c * 512:(c + 1) * 512],
                    in_=att_sb[:, cp * 512: cp * 512 + 512])
                att_guard[cp] = em.inc(st, "sy", 16)
        em.w(sy, "sy")
    return nc


def build_l2(dump=False):
    nc = bass.Bass("TRN2", target_bir_lowering=False)
    attnT = nc.dram_tensor("attnT", [DCN, 128, 1024], BF16, kind="ExternalInput")
    wo_t = nc.dram_tensor("wo_t", [32, 128, DCN, 128], F32, kind="ExternalInput")
    wf_t = nc.dram_tensor("wf_t", [32, 128, DCN, 128], F32, kind="ExternalInput")
    xq = nc.dram_tensor("xq", [DCN, 128, 1024], F32, kind="ExternalInput")
    bo_c = nc.dram_tensor("bo_c", [128, 32], F32, kind="ExternalInput")
    bf_c = nc.dram_tensor("bf_c", [128, 32], F32, kind="ExternalInput")
    at_t = nc.dram_tensor("at_t", [DCN, 128, 128], F32, kind="ExternalInput")
    bt_t = nc.dram_tensor("bt_t", [32, 128, 128], F32, kind="ExternalInput")
    gw_t = nc.dram_tensor("gw_t", [DCN, 128, 4], F32, kind="ExternalInput")
    gb_row = nc.dram_tensor("gb_row", [1, 4], F32, kind="ExternalInput")
    iota = nc.dram_tensor("iota", [128, 512], F32, kind="ExternalInput")
    ident = nc.dram_tensor("ident", [128, 128], F32, kind="ExternalInput")
    y_out = nc.dram_tensor("y_out", [DCN, 128, 1024], F32, kind="ExternalOutput")
    R1 = nc.dram_tensor("R1", [DCN, 128, 1024], F32, kind="Internal")
    XLN = nc.dram_tensor("XLN", [DCN, 128, 1024], F32, kind="Internal")
    R2 = nc.dram_tensor("R2", [DCN, 128, 1024], F32, kind="Internal")
    if dump:
        R1o = nc.dram_tensor("R1o", [DCN, 128, 1024], F32, kind="ExternalOutput")
        XLNo = nc.dram_tensor("XLNo", [DCN, 128, 1024], F32, kind="ExternalOutput")
        R2o = nc.dram_tensor("R2o", [DCN, 128, 1024], F32, kind="ExternalOutput")
        UMo = nc.dram_tensor("UMo", [128, 1024], F32, kind="ExternalOutput")
        GTo = nc.dram_tensor("GTo", [128, 32], F32, kind="ExternalOutput")
        USo = nc.dram_tensor("USo", [128, 1024], F32, kind="ExternalOutput")
        MRo = nc.dram_tensor("MRo", [128, 1024], F32, kind="ExternalOutput")
        RSo = nc.dram_tensor("RSo", [128, 8], F32, kind="ExternalOutput")
        STo = nc.dram_tensor("STo", [128, 16], F32, kind="ExternalOutput")
        DBo = nc.dram_tensor("DBo", [128, 64], F32, kind="ExternalOutput")

    with ExitStack() as ctx:
        E = ctx.enter_context
        big = E(nc.sbuf_tensor("big", [128, DCN * 1024], BF16))
        wpan = E(nc.sbuf_tensor("wpan", [128, 4096 * 2], BF16))
        xres = E(nc.sbuf_tensor("xres", [128, 512 * 4], F32))
        r1s = E(nc.sbuf_tensor("r1s", [128, 512 * 2], F32))
        r1b = E(nc.sbuf_tensor("r1b", [128, 512 * 2], BF16))
        sqb = E(nc.sbuf_tensor("sqb", [128, 512 * 2], BF16))
        ones_bf = E(nc.sbuf_tensor("ones_bf", [128, 128], BF16))
        ones1f = E(nc.sbuf_tensor("ones1f", [1, 128], F32))
        stat = E(nc.sbuf_tensor("stat", [128, 4608], F32))
        atp = E(nc.sbuf_tensor("atp", [128, DCN * 128], BF16))
        btp = E(nc.sbuf_tensor("btp", [128, 32 * 128], BF16))
        gwp = E(nc.sbuf_tensor("gwp", [128, DCN * 4], F32))
        gbr = E(nc.sbuf_tensor("gbr", [1, 4], F32))
        iop = E(nc.sbuf_tensor("iop", [128, 512], F32))
        idp = E(nc.sbuf_tensor("idp", [128, 128], F32))
        usb = E(nc.sbuf_tensor("usb", [128, 512 * 2], BF16))
        umb = E(nc.sbuf_tensor("umb", [128, 512 * 2], BF16))
        gts = E(nc.sbuf_tensor("gts", [128, 32], F32))
        rsel = E(nc.sbuf_tensor("rsel", [128, 8], F32))
        rrow = E(nc.sbuf_tensor("rrow", [1, 512 * 2], F32))
        mrow = E(nc.sbuf_tensor("mrow", [128, 512 * 2], BF16))
        evf = E(nc.sbuf_tensor("evf", [128, 512 * 4], F32))
        dbge = E(nc.sbuf_tensor("dbge", [128, 64], F32))
        ps = [E(nc.psum_tensor(f"ps{i}", [128, 512], F32)) for i in range(8)]
        em = Em(nc, ctx)
        t, v, sc, gp, sy = nc.tensor, nc.vector, nc.scalar, nc.gpsimd, nc.sync
        sc.preload_act_table(AF.Sqrt)

        em.inc(gp.dma_start(out=atp[:, :].rearrange("p (a b) -> p a b", a=DCN),
                            in_=at_t.ap().transpose([1, 0, 2])), "gp", 16)
        em.inc(gp.dma_start(out=btp[:, :].rearrange("p (a b) -> p a b", a=32),
                            in_=bt_t.ap().transpose([1, 0, 2])), "gp", 16)
        em.inc(gp.dma_start(out=gwp[:, :].rearrange("p (a b) -> p a b", a=DCN),
                            in_=gw_t.ap().transpose([1, 0, 2])), "gp", 16)
        em.inc(gp.dma_start(out=gbr[:, :], in_=gb_row[:, :]), "gp", 16)
        em.inc(gp.dma_start(out=iop[:, :], in_=iota[:, :]), "gp", 16)
        em.inc(gp.dma_start(out=idp[:, :], in_=ident[:, :]), "gp", 16)
        gp.memset(ones_bf[:, :], 1.0)
        gp.memset(ones1f[:, :], 1.0)
        lat = em.inc(gp.dma_start(out=big[:, :].rearrange("p (a b) -> p a b", a=DCN),
                                  in_=attnT.ap().transpose([1, 0, 2])), "gp", 16)
        em.inc(sy.dma_start(out=stat[:, 4096:4128], in_=bo_c[:, :]), "sy", 16)
        lbf = em.inc(sy.dma_start(out=stat[:, 4128:4160], in_=bf_c[:, :]), "sy", 16)
        BO, BFC = 4096, 4128

        def mainpass(wsrc, bias_off, spill, extra_lora, src_resid):
            m0, d0_ = em.cnt["mm"], em.cnt["dve"]
            pan_guard = [m0, m0]
            bank_guard = [d0_] * 4
            x_guard = [d0_] * 4
            rb_guard = [m0, m0]
            for o in range(32):
                pp = o % 2
                em.w(gp, "mm", pan_guard[pp])
                pd = wpan[:, pp * 4096:(pp + 1) * 4096].rearrange("p (a b) -> p a b", a=DCN)
                lw = em.inc(gp.dma_start(out=pd, in_=wsrc[o]), "gp", 16)
                for tc in range(2):
                    pb = (o * 2 + tc) % 4
                    xslot = (o * 2 + tc) % 4
                    em.w(sy, "dve", x_guard[xslot])
                    lxr = em.inc(sy.dma_start(
                        out=xres[:, xslot * 512:(xslot + 1) * 512],
                        in_=src_resid[o, :, tc * 512:(tc + 1) * 512]), "sy", 16)
                    em.w(t, "gp", lw)
                    em.w(t, "dve", bank_guard[pb])
                    for dc in range(DCN):
                        last = (dc == DCN - 1) and (extra_lora is None)
                        mm = t.matmul(
                            ps[pb][:, :],
                            wpan[:, pp * 4096 + dc * 128: pp * 4096 + (dc + 1) * 128],
                            big[:, dc * 1024 + tc * 512: dc * 1024 + tc * 512 + 512],
                            start=(dc == 0), stop=last)
                    if extra_lora is not None:
                        mm = t.matmul(ps[pb][:, :], btp[:, o * 128:(o + 1) * 128],
                                      umb[:, tc * 512:(tc + 1) * 512],
                                      start=False, stop=True)
                    mdone = em.inc(mm, "mm")
                    pan_guard[pp] = mdone
                    em.w(v, "mm", mdone)
                    em.w(v, "sy", lxr)
                    stt = v.scalar_tensor_tensor(
                        r1s[:, tc * 512:(tc + 1) * 512], ps[pb][:, :],
                        stat[:, bias_off + o: bias_off + o + 1],
                        xres[:, xslot * 512:(xslot + 1) * 512], ALU.add, ALU.add)
                    sdone = em.inc(stt, "dve")
                    bank_guard[pb] = sdone
                    x_guard[xslot] = sdone
                    em.w(sy, "dve", sdone)
                    em.inc(sy.dma_start(out=spill[o, :, tc * 512:(tc + 1) * 512],
                                        in_=r1s[:, tc * 512:(tc + 1) * 512]), "sy", 16)
                    em.w(v, "mm", rb_guard[tc])
                    cb = v.tensor_copy(r1b[:, tc * 512:(tc + 1) * 512],
                                       r1s[:, tc * 512:(tc + 1) * 512])
                    cdone = em.inc(cb, "dve")
                    em.w(sc, "dve", cdone)
                    sq = sc.activation(sqb[:, tc * 512:(tc + 1) * 512],
                                       r1s[:, tc * 512:(tc + 1) * 512], AF.Square)
                    qdone = em.inc(sq, "act")
                    em.w(t, "dve", cdone)
                    em.w(t, "act", qdone)
                    t.matmul(ps[4 + tc][:, :], ones_bf[:, :],
                             r1b[:, tc * 512:(tc + 1) * 512],
                             start=(o == 0), stop=(o == 31))
                    smm = t.matmul(ps[6 + tc][:, :], ones_bf[:, :],
                                   sqb[:, tc * 512:(tc + 1) * 512],
                                   start=(o == 0), stop=(o == 31))
                    rb_guard[tc] = em.inc(smm, "mm")

        def ln_coeffs():
            for tc in range(2):
                em.w(v, "mm")
                m = v.tensor_single_scalar(stat[:, tc * 512:(tc + 1) * 512],
                                           ps[4 + tc][:, :], 1.0 / 4096.0, ALU.mult)
                em.inc(m, "dve")
                mq = v.tensor_single_scalar(stat[:, 3072:3584],
                                            ps[6 + tc][:, :], 1.0 / 4096.0, ALU.mult)
                em.inc(mq, "dve")
                m2 = v.tensor_tensor(stat[:, 3584:4096],
                                     stat[:, tc * 512:(tc + 1) * 512],
                                     stat[:, tc * 512:(tc + 1) * 512], ALU.mult)
                em.inc(m2, "dve")
                var = v.tensor_tensor(stat[:, 3072:3584], stat[:, 3072:3584],
                                      stat[:, 3584:4096], ALU.subtract)
                em.inc(var, "dve")
                vepс = v.tensor_single_scalar(stat[:, 3072:3584], stat[:, 3072:3584],
                                              1e-5, ALU.add)
                vdone = em.inc(vepс, "dve")
                em.w(sc, "dve", vdone)
                sd = sc.activation(stat[:, 3584:4096], stat[:, 3072:3584],
                                   AF.Sqrt, bias=0.0, scale=1.0)
                sdone = em.inc(sd, "act")
                em.w(v, "act", sdone)
                rc = v.reciprocal(stat[:, 1024 + tc * 512: 1024 + (tc + 1) * 512],
                                  stat[:, 3584:4096])
                em.inc(rc, "dve")

        em.w(t, "gp", lat)
        em.w(v, "sy", lbf)
        mainpass(wo_t, BO, R1, None, xq)
        ln_coeffs()

        em.w(sy, "sy")
        xg = [0, 0]
        ug = 0
        for dc in range(DCN):
            for tc in range(2):
                em.w(sy, "dve", xg[tc])
                lr = em.inc(sy.dma_start(out=evf[:, tc * 512:(tc + 1) * 512],
                                         in_=R1[dc, :, tc * 512:(tc + 1) * 512]), "sy", 16)
                em.w(v, "sy", lr)
                sub = v.tensor_tensor(evf[:, 1024 + tc * 512: 1024 + (tc + 1) * 512],
                                      evf[:, tc * 512:(tc + 1) * 512],
                                      stat[:, tc * 512:(tc + 1) * 512], ALU.subtract)
                em.inc(sub, "dve")
                mul = v.tensor_tensor(evf[:, tc * 512:(tc + 1) * 512],
                                      evf[:, 1024 + tc * 512: 1024 + (tc + 1) * 512],
                                      stat[:, 1024 + tc * 512: 1024 + (tc + 1) * 512],
                                      ALU.mult)
                em.inc(mul, "dve")
                cb = v.tensor_copy(big[:, dc * 1024 + tc * 512: dc * 1024 + tc * 512 + 512],
                                   evf[:, tc * 512:(tc + 1) * 512])
                cdone = em.inc(cb, "dve")
                xg[tc] = cdone
                em.w(sy, "dve", cdone)
                em.inc(sy.dma_start(out=XLN[dc, :, tc * 512:(tc + 1) * 512],
                                    in_=evf[:, tc * 512:(tc + 1) * 512]), "sy", 16)
                em.w(t, "dve", cdone)
                if dc == 0:
                    em.w(t, "dve", ug)
                um = t.matmul(ps[2 + tc][:, :], atp[:, dc * 128:(dc + 1) * 128],
                              big[:, dc * 1024 + tc * 512: dc * 1024 + tc * 512 + 512],
                              start=(dc == 0), stop=(dc == DCN - 1))
                em.inc(um, "mm")
        for tc in range(2):
            em.w(v, "mm")
            uc = v.tensor_copy(usb[:, tc * 512:(tc + 1) * 512], ps[2 + tc][:, :])
            ug = em.inc(uc, "dve")

        em.w(sy, "sy")
        em.w(sy, "dve")
        em.w(t, "dve", ug)
        gl_guard = [0, 0]
        for dc in range(DCN):
            slot = dc % 2
            em.w(sy, "mm", gl_guard[slot])
            lx = em.inc(sy.dma_start(out=evf[:, slot * 1024:(slot + 1) * 1024],
                                     in_=XLN[dc]), "sy", 16)
            em.w(t, "sy", lx)
            for tc in range(2):
                for tl in range(4):
                    gmm = t.matmul(ps[tc * 4 + tl][:, 0:4],
                                   evf[:, slot * 1024 + tc * 512 + tl * 128:
                                       slot * 1024 + tc * 512 + (tl + 1) * 128],
                                   gwp[:, dc * 4:(dc + 1) * 4],
                                   start=(dc == 0), stop=False)
            gl_guard[slot] = em.inc(gmm, "mm")
        em.w(t, "gp")
        for tc in range(2):
            for tl in range(4):
                gmm = t.matmul(ps[tc * 4 + tl][:, 0:4], ones1f[0:1, :], gbr[0:1, :],
                               start=False, stop=True)
        gdone = em.inc(gmm, "mm")
        em.w(v, "mm", gdone)
        for tc in range(2):
            for tl in range(4):
                g4 = v.tensor_copy(gts[:, (tc * 4 + tl) * 4:(tc * 4 + tl) * 4 + 4],
                                   ps[tc * 4 + tl][:, 0:4])
                em.inc(g4, "dve")
        GS = 3072
        BL = list(range(8))
        col = lambda blk, k: stat[:, GS + blk * 16 + k: GS + blk * 16 + k + 1]
        for blk in BL:
            em.inc(v.reduce_max(col(blk, 0), gts[:, blk * 4: blk * 4 + 4], axis=AXX), "dve")
        for i in range(4):
            for blk in BL:
                em.inc(v.tensor_tensor(col(blk, 1 + i), gts[:, blk * 4 + i: blk * 4 + i + 1],
                                       col(blk, 0), ALU.is_ge), "dve")
        for blk in BL:
            em.inc(v.tensor_single_scalar(col(blk, 8), col(blk, 3), -32.0, ALU.mult), "dve")
        for blk in BL:
            em.inc(v.tensor_single_scalar(col(blk, 8), col(blk, 8), 64.0, ALU.add), "dve")
        for blk in BL:
            em.inc(v.tensor_single_scalar(col(blk, 9), col(blk, 8), 16.0, ALU.subtract), "dve")
        for blk in BL:
            em.inc(v.tensor_tensor(col(blk, 10), col(blk, 9), col(blk, 2), ALU.mult), "dve")
        for blk in BL:
            em.inc(v.tensor_tensor(col(blk, 11), col(blk, 8), col(blk, 10), ALU.subtract), "dve")
        for blk in BL:
            em.inc(v.tensor_single_scalar(col(blk, 12), col(blk, 11), 8.0, ALU.subtract), "dve")
        for blk in BL:
            em.inc(v.tensor_tensor(col(blk, 13), col(blk, 12), col(blk, 1), ALU.mult), "dve")
        for blk in BL:
            em.inc(v.tensor_tensor(rsel[:, blk: blk + 1], col(blk, 11), col(blk, 13),
                                   ALU.subtract), "dve")
        sdone = em.cnt["dve"]
        last_cp = sdone
        for blk in BL:
            tc, tl = blk // 4, blk % 4
            em.w(t, "dve", last_cp)
            tr = t.transpose(ps[0][0:1, 0:128], rsel[:, blk: blk + 1], idp[:, :])
            tdone = em.inc(tr, "mm")
            em.w(v, "mm", tdone)
            cp = v.tensor_copy(rrow[0:1, tc * 512 + tl * 128: tc * 512 + (tl + 1) * 128],
                               ps[0][0:1, 0:128])
            last_cp = em.inc(cp, "dve")
        if dump:
            for blk in [0, 1]:
                em.inc(v.tensor_copy(dbge[:, blk * 8: blk * 8 + 5],
                                     stat[:, GS + blk * 16: GS + blk * 16 + 5]), "dve")
        for tc in range(2):
            em.w(t, "dve")
            mr = t.matmul(ps[1][:, :], ones1f[0:1, :], rrow[0:1, tc * 512:(tc + 1) * 512],
                          start=True, stop=True)
            mrd = em.inc(mr, "mm")
            em.w(v, "mm", mrd)
            msk = v.tensor_tensor(mrow[:, tc * 512:(tc + 1) * 512],
                                  ps[1][:, :], iop[:, :], ALU.is_gt)
            em.inc(msk, "dve")
            um = v.tensor_tensor(umb[:, tc * 512:(tc + 1) * 512],
                                 usb[:, tc * 512:(tc + 1) * 512],
                                 mrow[:, tc * 512:(tc + 1) * 512], ALU.mult)
            em.inc(um, "dve")

        em.w(t, "dve")
        mainpass(wf_t, BFC, R2, True and (btp, umb), XLN)
        ln_coeffs()

        if dump:
            em.w(sy, "sy")
            em.w(gp, "dve")
            em.inc(gp.dma_start(out=R1o.ap(), in_=R1.ap()), "gp", 16)
            em.inc(gp.dma_start(out=XLNo.ap(), in_=XLN.ap()), "gp", 16)
            em.inc(gp.dma_start(out=R2o.ap(), in_=R2.ap()), "gp", 16)
            em.inc(gp.dma_start(out=UMo[:, 0:512], in_=umb[:, 0:512]), "gp", 16)
            em.inc(gp.dma_start(out=UMo[:, 512:1024], in_=umb[:, 512:1024]), "gp", 16)
            em.inc(gp.dma_start(out=GTo[:, :], in_=gts[:, :]), "gp", 16)
            em.inc(gp.dma_start(out=USo[:, 0:512], in_=usb[:, 0:512]), "gp", 16)
            em.inc(gp.dma_start(out=USo[:, 512:1024], in_=usb[:, 512:1024]), "gp", 16)
            em.inc(gp.dma_start(out=MRo[:, 0:512], in_=mrow[:, 0:512]), "gp", 16)
            em.inc(gp.dma_start(out=MRo[:, 512:1024], in_=mrow[:, 512:1024]), "gp", 16)
            em.inc(gp.dma_start(out=RSo[:, :], in_=rsel[:, :]), "gp", 16)
            em.inc(gp.dma_start(out=STo[:, :], in_=stat[:, 3072:3088]), "gp", 16)
            em.inc(gp.dma_start(out=DBo[:, :], in_=dbge[:, :]), "gp", 16)
            em.w(gp, "gp")
        em.w(sy, "sy")
        em.w(sy, "mm")
        em.w(sy, "dve")
        xg2 = [0, 0]
        for dc in range(DCN):
            for tc in range(2):
                em.w(sy, "dve", xg2[tc])
                lr = em.inc(sy.dma_start(out=evf[:, tc * 512:(tc + 1) * 512],
                                         in_=R2[dc, :, tc * 512:(tc + 1) * 512]), "sy", 16)
                em.w(v, "sy", lr)
                sub = v.tensor_tensor(evf[:, 1024 + tc * 512: 1024 + (tc + 1) * 512],
                                      evf[:, tc * 512:(tc + 1) * 512],
                                      stat[:, tc * 512:(tc + 1) * 512], ALU.subtract)
                em.inc(sub, "dve")
                mul = v.tensor_tensor(evf[:, tc * 512:(tc + 1) * 512],
                                      evf[:, 1024 + tc * 512: 1024 + (tc + 1) * 512],
                                      stat[:, 1024 + tc * 512: 1024 + (tc + 1) * 512],
                                      ALU.mult)
                mdone = em.inc(mul, "dve")
                xg2[tc] = mdone
                em.w(sy, "dve", mdone)
                em.inc(sy.dma_start(out=y_out[dc, :, tc * 512:(tc + 1) * 512],
                                    in_=evf[:, tc * 512:(tc + 1) * 512]), "sy", 16)
        em.w(sy, "sy")
    return nc


_NC_CACHE = {}


def kernel(**inputs):
    shared = _host_prep_shared(inputs)
    if "l1" not in _NC_CACHE:
        _NC_CACHE["l1"] = build_l1()
    maps1 = []
    for c in range(8):
        g, j = c // 4, c % 4
        maps1.append({
            "x_t": shared[f"x_t{g}"],
            "wq_t": shared[f"wq_t{j}"], "wk_t": shared[f"wk_t{j}"],
            "wv_t": shared[f"wv_t{j}"],
            "bqk": shared[f"bqk{j}"], "bv": shared[f"bv{j}"],
            "atab": shared[f"atab{j}"], "trimask": shared["trimask"],
        })
    res1 = run_bass_kernel_spmd(_NC_CACHE["l1"], maps1, core_ids=list(range(8)))
    contribs = [np.asarray(res1.results[c]["contrib"]) for c in range(8)]

    attnT_full = {g: np.concatenate([contribs[g * 4 + i] for i in range(4)], axis=0)
                  for g in range(B)}
    if "l2" not in _NC_CACHE:
        _NC_CACHE["l2"] = build_l2()
    maps2 = []
    for c in range(8):
        g, j = c // 4, c % 4
        sl = attnT_full[g][:, 1024 * j: 1024 * j + 1024]
        att_t = np.ascontiguousarray(sl.reshape(DCN, 128, 1024))
        maps2.append({
            "attnT": att_t, "wo_t": shared["wo_t"], "wf_t": shared["wf_t"],
            "xq": shared[f"xq{g}_{j}"], "bo_c": shared["bo_c"], "bf_c": shared["bf_c"],
            "at_t": shared["at_t"], "bt_t": shared["bt_t"], "gw_t": shared["gw_t"],
            "gb_row": shared["gb_row"], "iota": shared["iota"], "ident": shared["ident"],
        })
    res2 = run_bass_kernel_spmd(_NC_CACHE["l2"], maps2, core_ids=list(range(8)))
    out = np.empty((B, S, D), dtype=np.float32)
    for c in range(8):
        g, j = c // 4, c % 4
        y = np.asarray(res2.results[c]["y_out"])
        out[g, 1024 * j: 1024 * j + 1024, :] = y.transpose(2, 0, 1).reshape(1024, D)
    return out
